# version 01
# baseline (speedup 1.0000x reference)
import numpy as np
import ml_dtypes
from contextlib import ExitStack

import concourse.bass as bass
import concourse.mybir as mybir
from concourse.alu_op_type import AluOpType as Op
from concourse.bass_utils import run_bass_kernel_spmd

F32 = mybir.dt.float32
BF16 = mybir.dt.bfloat16
AF = mybir.ActivationFunctionType
BF = ml_dtypes.bfloat16

T, D, DI, NST, DTR = 16384, 1024, 2048, 16, 64
E, S, HID = 8, 512, 4096
NCORES = 8
TLOC = T // NCORES
CH = 512
TC = 1024
NTS = TC // 512
NCHUNK = T // TC
SCALE = float(D) ** 0.5


def cc(a):
    return np.ascontiguousarray(a)



def build_L1():
    nc = bass.Bass("TRN2", target_bir_lowering=False)
    xT = nc.dram_tensor("xT", [D, T], F32, kind="ExternalInput").ap()
    in_w = nc.dram_tensor("in_w_l", [D, 2 * CH], F32, kind="ExternalInput").ap()
    conv_w = nc.dram_tensor("conv_w_l", [128, 16], F32, kind="ExternalInput").ap()
    conv_b = nc.dram_tensor("conv_b_l", [128, 4], F32, kind="ExternalInput").ap()
    xp_w = nc.dram_tensor("xp_w_l", [CH, 96], F32, kind="ExternalInput").ap()

    o_xc = nc.dram_tensor("xc", [CH, T], BF16, kind="ExternalOutput").ap()
    o_zs = nc.dram_tensor("zs", [CH, T], BF16, kind="ExternalOutput").ap()
    o_dtp = nc.dram_tensor("dtp_p", [DTR, T], F32, kind="ExternalOutput").ap()
    o_B = nc.dram_tensor("B_p", [NST, T], F32, kind="ExternalOutput").ap()
    o_C = nc.dram_tensor("C_p", [NST, T], F32, kind="ExternalOutput").ap()
    owt_i = nc.dram_tensor("owt", [D, CH], F32, kind="ExternalInput").ap()
    projh_i = nc.dram_tensor("projh", [D, D], F32, kind="ExternalInput").ap()
    o_wc = nc.dram_tensor("wc", [CH, D], BF16, kind="ExternalOutput").ap()

    xT_r = xT.rearrange("(a p) t -> p a t", p=128)
    in_w_r = in_w.rearrange("(a p) c -> p a c", p=128)
    xp_w_r = xp_w.rearrange("(a p) c -> p a c", p=128)
    o_xc_r = o_xc.rearrange("(a p) t -> p a t", p=128)
    o_zs_r = o_zs.rearrange("(a p) t -> p a t", p=128)

    es = ExitStack()
    sb = lambda name, shape, dt: es.enter_context(nc.sbuf_tensor(name, shape, dt))

    xt32 = [sb(f"xt32_{i}", [128, 8, TC], F32) for i in range(2)]
    xtb = [sb(f"xtb_{i}", [128, 8, TC], BF16) for i in range(2)]
    wb = sb("wb", [128, 8, 2 * CH], BF16)
    xpw32 = sb("xpw32", [128, 4, 96], F32)
    xpwb = sb("xpwb", [128, 4, 96], BF16)
    cw = sb("cw", [128, 16], F32)
    cb = sb("cb", [128, 4], F32)
    xi = [sb(f"xi_{i}", [128, 4, TC + 3], BF16) for i in range(2)]
    xcb = sb("xcb", [128, 4, TC], BF16)
    zsb = sb("zsb", [128, 4, TC], BF16)
    cv = [sb(f"cv_{i}", [128, TC], BF16) for i in range(4)]
    cpre = sb("cpre", [128, 4, TC], BF16)
    dbc = sb("dbc", [96, TC], F32)
    owtb = sb("owtb", [128, 8, CH], BF16)
    projb2 = sb("projb2", [128, 8, D], BF16)
    wcl = sb("wcl", [128, 4, D], BF16)
    pz = [es.enter_context(nc.psum_tensor(f"pz_{i}", [128, 512], F32)) for i in range(4)]
    pd = es.enter_context(nc.psum_tensor("pd", [96, 512], F32))
    pw = es.enter_context(nc.psum_tensor("pw", [128, 512], F32))

    sIn = es.enter_context(nc.semaphore("sIn"))
    sPe = es.enter_context(nc.semaphore("sPe"))
    sAct = es.enter_context(nc.semaphore("sAct"))
    sDve = es.enter_context(nc.semaphore("sDve"))
    sOut = es.enter_context(nc.semaphore("sOut"))
    sOxc = es.enter_context(nc.semaphore("sOxc"))
    sOzs = es.enter_context(nc.semaphore("sOzs"))
    sOdbc = es.enter_context(nc.semaphore("sOdbc"))
    sIn2 = es.enter_context(nc.semaphore("sIn2"))
    sPe2 = es.enter_context(nc.semaphore("sPe2"))
    sAct2 = es.enter_context(nc.semaphore("sAct2"))
    sDve2 = es.enter_context(nc.semaphore("sDve2"))
    sOut2 = es.enter_context(nc.semaphore("sOut2"))
    blk = es.enter_context(nc.Block())

    owt_r = owt_i.rearrange("(a p) c -> p a c", p=128)
    projh_r = projh_i.rearrange("(a p) c -> p a c", p=128)
    o_wc_r = o_wc.rearrange("(a p) c -> p a c", p=128)

    NXG = 8 * NTS
    PEC = NXG + NTS
    ACC = NXG + 4 + NTS
    def v_cast(c): return 3 + 6 * c + 1
    def v_conv(c, dt): return 3 + 6 * c + 2 + dt
    def v_marg(c): return 3 + 6 * c + 6
    def v_xzg(c, g): return PEC * c + g + 1
    def v_dbcg(c, ts): return PEC * c + NXG + ts + 1
    def v_evac(c, i): return ACC * c + i + 1
    def v_silu(c, dt): return ACC * c + NXG + dt + 1
    def v_dbce(c, ts): return ACC * c + NXG + 4 + ts + 1

    @blk.sync
    def _(sy):
        for j in range(16):
            if j >= 1:
                sy.wait_ge(sIn2, 16 * j)
            if j >= 2:
                sy.wait_ge(sDve2, j - 1)
            if j < 8:
                sy.dma_start(out=xt32[j % 2][:, 0, 0:CH], in_=owt_r[:, j, :]).then_inc(sIn2, 16)
            else:
                sy.dma_start(out=xt32[j % 2][:, 0, :], in_=projh_r[:, j - 8, :]).then_inc(sIn2, 16)
        sy.wait_ge(sDve2, 16)
        sy.dma_start(out=xt32[0][:], in_=in_w_r).then_inc(sIn, 16)
        sy.dma_start(out=cw[:], in_=conv_w).then_inc(sIn, 16)
        sy.dma_start(out=cb[:], in_=conv_b).then_inc(sIn, 16)
        sy.dma_start(out=xpw32[:], in_=xp_w_r).then_inc(sIn, 16)
        for c in range(NCHUNK):
            sy.wait_ge(sIn, 64 + 16 * c)
            if c == 0:
                sy.wait_ge(sDve, 1)
            if c >= 2:
                sy.wait_ge(sDve, v_cast(c - 2))
            sy.dma_start(out=xt32[c % 2][:], in_=xT_r[:, :, c * TC:(c + 1) * TC]
                         ).then_inc(sIn, 16)

    @blk.vector
    def _(v):
        for j in range(16):
            v.wait_ge(sIn2, 16 * (j + 1))
            if j < 8:
                v.tensor_copy(owtb[:, j, :], xt32[j % 2][:, 0, 0:CH]).then_inc(sDve2, 1)
            else:
                v.tensor_copy(projb2[:, j - 8, :], xt32[j % 2][:, 0, :]).then_inc(sDve2, 1)
        v.wait_ge(sIn, 64)
        v.tensor_copy(wb[:], xt32[0][:]).then_inc(sDve, 1)
        v.tensor_copy(xpwb[:], xpw32[:]).then_inc(sDve, 1)
        v.memset(xi[0][:, :, 0:3], 0.0).then_inc(sDve, 1)
        for c in range(NCHUNK):
            v.wait_ge(sIn, 64 + 16 * (c + 1))
            v.tensor_copy(xtb[c % 2][:], xt32[c % 2][:]).then_inc(sDve, 1)
            if c >= 1:
                v.wait_ge(sAct, v_silu(c - 1, 3))
            for dt in range(4):
                v.wait_ge(sAct, v_evac(c, NTS * dt + NTS - 1))
                for k in range(4):
                    v.tensor_scalar(out=cv[k][:], in0=xi[c % 2][:, dt, k:k + TC],
                                    scalar1=cw[:, 4 * dt + k:4 * dt + k + 1],
                                    scalar2=None, op0=Op.mult, op1=Op.bypass)
                v.tensor_tensor(out=cv[0][:], in0=cv[0][:], in1=cv[1][:], op=Op.add)
                v.tensor_tensor(out=cv[2][:], in0=cv[2][:], in1=cv[3][:], op=Op.add)
                v.tensor_tensor(out=cpre[:, dt, :], in0=cv[0][:], in1=cv[2][:],
                                op=Op.add).then_inc(sDve, 1)
            if c < NCHUNK - 1:
                ins = None
                for dt in range(4):
                    ins = v.tensor_copy(xi[(c + 1) % 2][:, dt, 0:3],
                                        xi[c % 2][:, dt, TC:TC + 3])
                ins.then_inc(sDve, 1)
            else:
                v.nop().then_inc(sDve, 1)

    @blk.tensor
    def _(tn):
        tn.wait_ge(sDve2, 16)
        for cht in range(4):
            for oh in range(2):
                gw = cht * 2 + oh
                if gw >= 1:
                    tn.wait_ge(sAct2, gw)
                ins = None
                for mt in range(8):
                    ins = tn.matmul(pw[:], owtb[:, mt, cht * 128:(cht + 1) * 128],
                                    projb2[:, mt, oh * 512:(oh + 1) * 512],
                                    start=(mt == 0), stop=(mt == 7))
                ins.then_inc(sPe2, 1)
        tn.wait_ge(sDve, 2)
        for c in range(NCHUNK):
            tn.wait_ge(sDve, v_cast(c))
            for oc in range(8):
                for ts in range(NTS):
                    g = oc * NTS + ts
                    G = NXG * c + g
                    if G >= 4:
                        cp, ip = (G - 4) // NXG, (G - 4) % NXG
                        tn.wait_ge(sAct, v_evac(cp, ip))
                    ins = None
                    for k in range(8):
                        ins = tn.matmul(pz[g % 4][:],
                                        wb[:, k, oc * 128:(oc + 1) * 128],
                                        xtb[c % 2][:, k, ts * 512:(ts + 1) * 512],
                                        start=(k == 0), stop=(k == 7))
                    ins.then_inc(sPe, 1)
            for ts in range(NTS):
                tn.wait_ge(sAct, v_silu(c, 3))
                if ts >= 1:
                    tn.wait_ge(sAct, v_dbce(c, ts - 1))
                ins = None
                for k in range(4):
                    ins = tn.matmul(pd[:], xpwb[:, k, :],
                                    xcb[:, k, ts * 512:(ts + 1) * 512],
                                    start=(k == 0), stop=(k == 3))
                ins.then_inc(sPe, 1)

    @blk.scalar
    def _(sc):
        for gw in range(8):
            cht, oh = gw // 2, gw % 2
            sc.wait_ge(sPe2, gw + 1)
            sc.activation(out=wcl[:, cht, oh * 512:(oh + 1) * 512], in_=pw[:],
                          func=AF.Copy).then_inc(sAct2, 1)
        for c in range(NCHUNK):
            for oc in range(8):
                for ts in range(NTS):
                    g = oc * NTS + ts
                    sc.wait_ge(sPe, v_xzg(c, g))
                    if c >= 2 and oc < 4:
                        sc.wait_ge(sDve, v_marg(c - 1))
                    if c >= 1 and oc >= 4:
                        sc.wait_ge(sOzs, 16 * c)
                    if oc < 4:
                        sc.activation(out=xi[c % 2][:, oc, 3 + ts * 512:3 + (ts + 1) * 512],
                                      in_=pz[g % 4][:], func=AF.Copy
                                      ).then_inc(sAct, 1)
                    else:
                        sc.activation(out=zsb[:, oc - 4, ts * 512:(ts + 1) * 512],
                                      in_=pz[g % 4][:], func=AF.Silu
                                      ).then_inc(sAct, 1)
            for dt in range(4):
                sc.wait_ge(sDve, v_conv(c, dt))
                if c >= 1:
                    sc.wait_ge(sOxc, 16 * c)
                sc.activation(out=xcb[:, dt, :], in_=cpre[:, dt, :], func=AF.Silu,
                              bias=cb[:, dt:dt + 1]).then_inc(sAct, 1)
            for ts in range(NTS):
                sc.wait_ge(sPe, v_dbcg(c, ts))
                if c >= 1:
                    sc.wait_ge(sOdbc, 48 * c)
                sc.activation(out=dbc[:, ts * 512:(ts + 1) * 512], in_=pd[:],
                              func=AF.Copy).then_inc(sAct, 1)

    @blk.gpsimd
    def _(g):
        g.wait_ge(sAct2, 8)
        g.dma_start(out=o_wc_r[:], in_=wcl[:]).then_inc(sOut2, 16)
        for c in range(NCHUNK):
            sl = slice(c * TC, (c + 1) * TC)
            g.wait_ge(sOxc, 16 * c)
            g.wait_ge(sOzs, 16 * c)
            g.wait_ge(sOdbc, 48 * c)
            g.wait_ge(sAct, v_silu(c, 3))
            g.dma_start(out=o_xc_r[:, :, sl], in_=xcb[:]).then_inc(sOxc, 16)
            g.wait_ge(sAct, v_evac(c, NXG - 1))
            g.dma_start(out=o_zs_r[:, :, sl], in_=zsb[:]).then_inc(sOzs, 16)
            g.wait_ge(sAct, v_dbce(c, NTS - 1))
            g.dma_start(out=o_dtp[:, sl], in_=dbc[0:64, :]).then_inc(sOdbc, 16)
            g.dma_start(out=o_B[:, sl], in_=dbc[64:80, :]).then_inc(sOdbc, 16)
            g.dma_start(out=o_C[:, sl], in_=dbc[80:96, :]).then_inc(sOdbc, 16)
        g.wait_ge(sOxc, 16 * NCHUNK)
        g.wait_ge(sOzs, 16 * NCHUNK)
        g.wait_ge(sOdbc, 48 * NCHUNK)
        g.wait_ge(sOut2, 16)

    es.close()
    return nc


def host_L1_maps(inp):
    x2d = np.asarray(inp["L0_attn_x"], np.float32)[0]
    in_w = np.asarray(inp["L0_attn_in_w"], np.float32)
    conv_w = np.asarray(inp["L0_attn_conv_w"], np.float32)
    conv_b = np.asarray(inp["L0_attn_conv_b"], np.float32)
    xp_w = np.asarray(inp["L0_attn_xp_w"], np.float32)
    out_w = np.asarray(inp["L0_attn_out_w"], np.float32)
    proj_w = np.asarray(inp["L0_attn_proj_w"], np.float32)
    xT_f = cc(x2d.T)
    xT_b = cc(x2d[::-1].T)
    maps = []
    for c in range(NCORES):
        d = 0 if c < 4 else 1
        ci = 512 * (c % 4)
        maps.append(dict(
            xT=xT_f if d == 0 else xT_b,
            in_w_l=cc(np.concatenate([in_w[d][:, ci:ci + 512],
                                      in_w[d][:, DI + ci:DI + ci + 512]], axis=1)),
            conv_w_l=cc(conv_w[d][ci:ci + 512].reshape(4, 128, 4)
                        .transpose(1, 0, 2).reshape(128, 16)),
            conv_b_l=cc(conv_b[d][ci:ci + 512].reshape(4, 128).T),
            xp_w_l=cc(xp_w[d][ci:ci + 512]),
            owt=cc(out_w[d][ci:ci + 512].T),
            projh=cc(proj_w[d * D:(d + 1) * D]),
        ))
    return maps





def build_L2():
    nc = bass.Bass("TRN2", target_bir_lowering=False)
    xc_i = nc.dram_tensor("xc", [CH, T], BF16, kind="ExternalInput").ap()
    zs_i = nc.dram_tensor("zs", [CH, T], BF16, kind="ExternalInput").ap()
    dtp4 = nc.dram_tensor("dtp4", [4, DTR, T], F32, kind="ExternalInput").ap()
    B4 = nc.dram_tensor("B4", [4, NST, T], F32, kind="ExternalInput").ap()
    C4 = nc.dram_tensor("C4", [4, NST, T], F32, kind="ExternalInput").ap()
    dtw_i = nc.dram_tensor("dtw", [DTR, CH], F32, kind="ExternalInput").ap()
    dtb_i = nc.dram_tensor("dtb", [128, 4], F32, kind="ExternalInput").ap()
    alog_i = nc.dram_tensor("alog", [128, 64], F32, kind="ExternalInput").ap()
    dd_i = nc.dram_tensor("dd", [128, 4], F32, kind="ExternalInput").ap()
    sel_i = nc.dram_tensor("sel", [NST, NST, 128], BF16, kind="ExternalInput").ap()
    o_y = nc.dram_tensor("y", [CH, T], BF16, kind="ExternalOutput").ap()

    xc_r = xc_i.rearrange("(a p) t -> p a t", p=128)
    zs_r = zs_i.rearrange("(a p) t -> p a t", p=128)
    o_y_r = o_y.rearrange("(a p) t -> p a t", p=128)

    es = ExitStack()
    sb = lambda name, shape, dt: es.enter_context(nc.sbuf_tensor(name, shape, dt))

    xcc = [sb(f"xcc{i}", [128, 4, TC], BF16) for i in range(2)]
    zsc = [sb(f"zsc{i}", [128, 4, TC], BF16) for i in range(2)]
    dparts = sb("dparts", [DTR, 4, TC], F32)
    bparts = sb("bparts", [NST, 4, TC], F32)
    cparts = sb("cparts", [NST, 4, TC], F32)
    dsum = sb("dsum", [DTR, TC], F32)
    dsum_b = sb("dsum_b", [DTR, TC], BF16)
    bsum = sb("bsum", [NST, TC], F32)
    bsum_b = sb("bsum_b", [NST, TC], BF16)
    csum = sb("csum", [NST, TC], F32)
    csum_b = sb("csum_b", [NST, TC], BF16)
    dt32 = sb("dt32", [128, 4, TC], F32)
    dtbf = sb("dtbf", [128, 4, TC], BF16)
    wpr = sb("wpr", [128, 4, TC], BF16)
    absu = sb("absu", [128, TC], F32)
    eet = sb("eet", [128, TC], F32)
    llt = sb("llt", [128, TC], F32)
    brep = [sb(f"brep{i}", [128, TC], BF16) for i in range(2)]
    crep = [sb(f"crep{i}", [128, TC], BF16) for i in range(2)]
    dA = [sb(f"dA{i}", [128, TC], BF16) for i in range(2)]
    u_t = sb("u_t", [128, TC], BF16)
    hh = sb("hh", [128, TC], BF16)
    hc_t = sb("hc_t", [128, TC], BF16)
    yacc = sb("yacc", [128, 4, TC], F32)
    ybuf = sb("ybuf", [128, 4, TC], BF16)
    carry = sb("carry", [128, 64], F32)
    dtw32 = sb("dtw32", [DTR, CH], F32)
    dtwb = sb("dtwb", [DTR, CH], BF16)
    dtb_s = sb("dtb_s", [128, 4], F32)
    ones_s = sb("ones_s", [128, 1], F32)
    alog_s = sb("alog_s", [128, 64], F32)
    aexp_s = sb("aexp_s", [128, 64], F32)
    a_s = sb("a_s", [128, 64], F32)
    dd_s = sb("dd_s", [128, 4], F32)
    sel_s = sb("sel_s", [NST, NST, 128], BF16)

    pdt = es.enter_context(nc.psum_tensor("pdt", [128, 512], F32))
    prb = es.enter_context(nc.psum_tensor("prb", [128, TC], F32))
    prc = es.enter_context(nc.psum_tensor("prc", [128, TC], F32))

    sIn = es.enter_context(nc.semaphore("sIn"))
    sPe = es.enter_context(nc.semaphore("sPe"))
    sAct = es.enter_context(nc.semaphore("sAct"))
    sDve = es.enter_context(nc.semaphore("sDve"))
    sOut = es.enter_context(nc.semaphore("sOut"))
    sVc = es.enter_context(nc.semaphore("sVc"))
    blk = es.enter_context(nc.Block())

    NG = 64
    DVC = 1 + 4 + NG + 1
    ACC2 = 8 + 4 + 16 + NG
    PEC2 = 8 + 16

    def v_cast(c): return 2 + DVC * c + 1
    def v_wp(c, cht): return 2 + DVC * c + 2 + cht
    def v_accg(c, g): return 2 + DVC * c + 6 + g
    def v_yfin(c): return 2 + DVC * c + DVC
    def v_dte(c, g): return 1 + ACC2 * c + g + 1
    def v_ll(c, cht): return 1 + ACC2 * c + 9 + cht
    def v_repev(c, n): return 1 + ACC2 * c + 13 + 5 * n
    def v_dAg(c, n, cht): return 1 + ACC2 * c + 14 + 5 * n + cht
    def v_dtg(c, g): return PEC2 * c + g + 1
    def v_repg(c, n): return PEC2 * c + 9 + n
    def v_in(c): return 80 + 224 * (c + 1)

    @blk.sync
    def _(sy):
        sy.dma_start(out=dtw32[:], in_=dtw_i).then_inc(sIn, 16)
        sy.dma_start(out=dtb_s[:], in_=dtb_i).then_inc(sIn, 16)
        sy.dma_start(out=alog_s[:], in_=alog_i).then_inc(sIn, 16)
        sy.dma_start(out=dd_s[:], in_=dd_i).then_inc(sIn, 16)
        sy.dma_start(out=sel_s[:], in_=sel_i).then_inc(sIn, 16)
        for c in range(NCHUNK):
            sl = slice(c * TC, (c + 1) * TC)
            sy.wait_ge(sIn, 80 + 224 * c)
            if c >= 2:
                sy.wait_ge(sDve, v_yfin(c - 2))
            if c >= 1:
                sy.wait_ge(sDve, v_cast(c - 1))
            sy.dma_start(out=xcc[c % 2][:], in_=xc_r[:, :, sl]).then_inc(sIn, 16)
            sy.dma_start(out=zsc[c % 2][:], in_=zs_r[:, :, sl]).then_inc(sIn, 16)
            for i in range(4):
                sy.dma_start(out=dparts[:, i, :], in_=dtp4[i, :, sl]).then_inc(sIn, 16)
            for i in range(4):
                sy.dma_start(out=bparts[:, i, :], in_=B4[i, :, sl]).then_inc(sIn, 16)
            for i in range(4):
                sy.dma_start(out=cparts[:, i, :], in_=C4[i, :, sl]).then_inc(sIn, 16)

    @blk.vector
    def _(v):
        v.wait_ge(sIn, 80)
        v.memset(ones_s[:], 1.0)
        v.tensor_copy(dtwb[:], dtw32[:]).then_inc(sDve, 1)
        v.wait_ge(sAct, 1)
        v.tensor_scalar(out=a_s[:], in0=aexp_s[:], scalar1=-1.0, scalar2=None,
                        op0=Op.mult, op1=Op.bypass).then_inc(sDve, 1)
        v.memset(carry[:], 0.0)
        for c in range(NCHUNK):
            v.wait_ge(sIn, v_in(c))
            v.tensor_tensor(out=dsum[:], in0=dparts[:, 0, :], in1=dparts[:, 1, :], op=Op.add)
            v.tensor_tensor(out=dsum[:], in0=dsum[:], in1=dparts[:, 2, :], op=Op.add)
            v.tensor_tensor(out=dsum[:], in0=dsum[:], in1=dparts[:, 3, :], op=Op.add)
            v.tensor_tensor(out=bsum[:], in0=bparts[:, 0, :], in1=bparts[:, 1, :], op=Op.add)
            v.tensor_tensor(out=bsum[:], in0=bsum[:], in1=bparts[:, 2, :], op=Op.add)
            v.tensor_tensor(out=bsum[:], in0=bsum[:], in1=bparts[:, 3, :], op=Op.add)
            v.tensor_tensor(out=csum[:], in0=cparts[:, 0, :], in1=cparts[:, 1, :], op=Op.add)
            v.tensor_tensor(out=csum[:], in0=csum[:], in1=cparts[:, 2, :], op=Op.add)
            v.tensor_tensor(out=csum[:], in0=csum[:], in1=cparts[:, 3, :], op=Op.add)
            v.tensor_copy(dsum_b[:], dsum[:])
            v.tensor_copy(bsum_b[:], bsum[:])
            v.tensor_copy(csum_b[:], csum[:]).then_inc(sDve, 1)
            for cht in range(4):
                v.wait_ge(sAct, v_ll(c, cht))
                v.tensor_scalar(out=eet[:], in0=dt32[:, cht, :], scalar1=0.0,
                                scalar2=None, op0=Op.max, op1=Op.bypass)
                v.tensor_tensor(out=dt32[:, cht, :], in0=eet[:], in1=llt[:], op=Op.add)
                v.tensor_copy(dtbf[:, cht, :], dt32[:, cht, :])
                v.tensor_tensor(out=wpr[:, cht, :], in0=dtbf[:, cht, :],
                                in1=xcc[c % 2][:, cht, :], op=Op.mult).then_inc(sDve, 1)
            v.memset(yacc[:], 0.0)
            for n in range(NST):
                for cht in range(4):
                    g = n * 4 + cht
                    v.wait_ge(sAct, v_dAg(c, n, cht))
                    v.tensor_tensor(out=u_t[:], in0=wpr[:, cht, :],
                                    in1=brep[n % 2][:], op=Op.mult)
                    v.tensor_tensor_scan(out=hh[:], data0=dA[g % 2][:], data1=u_t[:],
                                         initial=carry[:, cht * 16 + n:cht * 16 + n + 1],
                                         op0=Op.mult, op1=Op.add).then_inc(sVc, 1)
                    v.wait_ge(sVc, NG * c + g + 1)
                    v.tensor_copy(carry[:, cht * 16 + n:cht * 16 + n + 1], hh[:, TC - 1:TC])
                    v.tensor_tensor(out=hc_t[:], in0=hh[:], in1=crep[n % 2][:], op=Op.mult)
                    v.tensor_tensor(out=yacc[:, cht, :], in0=yacc[:, cht, :],
                                    in1=hc_t[:], op=Op.add).then_inc(sDve, 1)
            if c >= 1:
                v.wait_ge(sOut, 16 * c)
            ins = None
            for cht in range(4):
                v.scalar_tensor_tensor(out=hc_t[:], in0=xcc[c % 2][:, cht, :],
                                       scalar=dd_s[:, cht:cht + 1], in1=yacc[:, cht, :],
                                       op0=Op.mult, op1=Op.add)
                ins = v.tensor_tensor(out=ybuf[:, cht, :], in0=hc_t[:],
                                      in1=zsc[c % 2][:, cht, :], op=Op.mult)
            ins.then_inc(sDve, 1)

    @blk.scalar
    def _(sc):
        sc.wait_ge(sIn, 80)
        sc.wait_ge(sDve, 1)
        sc.activation(out=aexp_s[:], in_=alog_s[:], func=AF.Exp).then_inc(sAct, 1)
        for c in range(NCHUNK):
            for cht in range(4):
                for ts in range(NTS):
                    g = cht * NTS + ts
                    sc.wait_ge(sPe, v_dtg(c, g))
                    sc.activation(out=dt32[:, cht, ts * 512:(ts + 1) * 512], in_=pdt[:],
                                  func=AF.Identity, bias=dtb_s[:, cht:cht + 1]
                                  ).then_inc(sAct, 1)
            for cht in range(4):
                sc.activation(out=absu[:], in_=dt32[:, cht, :], func=AF.Abs)
                sc.activation(out=eet[:], in_=absu[:], func=AF.Exp, scale=-1.0)
                sc.activation(out=llt[:], in_=eet[:], func=AF.Ln,
                              bias=ones_s[:, 0:1]).then_inc(sAct, 1)
            for n in range(NST):
                sc.wait_ge(sPe, v_repg(c, n))
                G0 = (16 * c + n) - 2
                if G0 >= 0:
                    cp, np_ = G0 // 16, G0 % 16
                    sc.wait_ge(sDve, v_accg(cp, np_ * 4 + 3))
                sc.activation(out=brep[n % 2][:], in_=prb[:], func=AF.Copy)
                sc.activation(out=crep[n % 2][:], in_=prc[:], func=AF.Copy
                              ).then_inc(sAct, 1)
                for cht in range(4):
                    G = NG * c + n * 4 + cht
                    if G >= 2:
                        cp, gp = (G - 2) // NG, (G - 2) % NG
                        sc.wait_ge(sDve, v_accg(cp, gp))
                    sc.activation(out=dA[(n * 4 + cht) % 2][:], in_=dt32[:, cht, :],
                                  func=AF.Exp, scale=a_s[:, cht * 16 + n:cht * 16 + n + 1]
                                  ).then_inc(sAct, 1)

    @blk.tensor
    def _(tn):
        tn.wait_ge(sIn, 80)
        tn.wait_ge(sDve, 1)
        for c in range(NCHUNK):
            tn.wait_ge(sDve, v_cast(c))
            for cht in range(4):
                for ts in range(NTS):
                    g = cht * NTS + ts
                    if not (c == 0 and g == 0):
                        cp, gp = (c, g - 1) if g >= 1 else (c - 1, 7)
                        tn.wait_ge(sAct, v_dte(cp, gp))
                    tn.matmul(pdt[:], dtwb[:, cht * 128:(cht + 1) * 128],
                              dsum_b[:, ts * 512:(ts + 1) * 512],
                              start=True, stop=True).then_inc(sPe, 1)
            for n in range(NST):
                if not (c == 0 and n == 0):
                    cp, np_ = (c, n - 1) if n >= 1 else (c - 1, 15)
                    tn.wait_ge(sAct, v_repev(cp, np_))
                for half in range(2):
                    tn.matmul(prb[:, half * 512:(half + 1) * 512], sel_s[:, n, :],
                              bsum_b[:, half * 512:(half + 1) * 512],
                              start=True, stop=True)
                ins = None
                for half in range(2):
                    ins = tn.matmul(prc[:, half * 512:(half + 1) * 512], sel_s[:, n, :],
                                    csum_b[:, half * 512:(half + 1) * 512],
                                    start=True, stop=True)
                ins.then_inc(sPe, 1)

    @blk.gpsimd
    def _(g):
        for c in range(NCHUNK):
            sl = slice(c * TC, (c + 1) * TC)
            g.wait_ge(sOut, 16 * c)
            g.wait_ge(sDve, v_yfin(c))
            g.dma_start(out=o_y_r[:, :, sl], in_=ybuf[:]).then_inc(sOut, 16)
        g.wait_ge(sOut, 16 * NCHUNK)

    es.close()
    return nc


def host_L2_maps(inp, r1):
    dt_w = np.asarray(inp["L0_attn_dt_w"], np.float32)
    dt_b = np.asarray(inp["L0_attn_dt_b"], np.float32)
    A_log = np.asarray(inp["L0_attn_A_log"], np.float32)
    Dsk = np.asarray(inp["L0_attn_D"], np.float32)
    sel = np.zeros((NST, NST, 128), BF)
    for n in range(NST):
        sel[n, n, :] = BF(1.0)
    maps = []
    for c in range(NCORES):
        d = 0 if c < 4 else 1
        ci = 512 * (c % 4)
        grp = range(0, 4) if d == 0 else range(4, 8)
        maps.append(dict(
            xc=r1.results[c]["xc"],
            zs=r1.results[c]["zs"],
            dtp4=cc(np.stack([r1.results[j]["dtp_p"] for j in grp])),
            B4=cc(np.stack([r1.results[j]["B_p"] for j in grp])),
            C4=cc(np.stack([r1.results[j]["C_p"] for j in grp])),
            dtw=cc(dt_w[d][:, ci:ci + 512]),
            dtb=cc(dt_b[d][ci:ci + 512].reshape(4, 128).T),
            alog=cc(A_log[d][ci:ci + 512].reshape(4, 128, NST)
                    .transpose(1, 0, 2).reshape(128, 64)),
            dd=cc(Dsk[d][ci:ci + 512].reshape(4, 128).T),
            sel=sel,
        ))
    return maps





def build_L3():
    nc = bass.Bass("TRN2", target_bir_lowering=False)
    y_i = nc.dram_tensor("y_all", [4 * CH * 2, TLOC], BF16, kind="ExternalInput").ap()
    wc_i = nc.dram_tensor("wc_all", [4 * CH * 2, D], BF16, kind="ExternalInput").ap()
    x_i = nc.dram_tensor("x_sl", [TLOC, D], F32, kind="ExternalInput").ap()
    pb_i = nc.dram_tensor("pb", [128, 8], F32, kind="ExternalInput").ap()
    eye_i = nc.dram_tensor("eyef", [128, 128], F32, kind="ExternalInput").ap()
    o_xm = nc.dram_tensor("x_mid", [TLOC, D], F32, kind="ExternalOutput").ap()

    NK = (4 * CH * 2) // 128
    y_r = y_i.rearrange("(k p) t -> p k t", p=128)
    wc_r = wc_i.rearrange("(k p) c -> p k c", p=128)
    x_r = x_i.rearrange("(a p) c -> p a c", p=128)
    o_r = o_xm.rearrange("(a p) c -> p a c", p=128)

    es = ExitStack()
    sb = lambda name, shape, dt: es.enter_context(nc.sbuf_tensor(name, shape, dt))

    wcb = sb("wcb", [128, NK, D], BF16)
    yh = sb("yh", [128, NK, TLOC // 2], BF16)
    xh = sb("xh", [128, 8, D], F32)
    xmh = sb("xmh", [128, 8, D], F32)
    scr = [sb(f"scr{i}", [128, 512], F32) for i in range(2)]
    pb_s = sb("pb_s", [128, 8], F32)
    eye_s = sb("eye_s", [128, 128], F32)
    sumx = sb("sumx", [128, 8], F32)
    sumsq = sb("sumsq", [128, 8], F32)
    mu = sb("mu", [128, 8], F32)
    mu2 = sb("mu2", [128, 8], F32)
    var = sb("var", [128, 8], F32)
    rstd = sb("rstd", [128, 8], F32)
    negmu = sb("negmu", [128, 8], F32)
    eps_s2 = sb("eps_s2", [128, 1], F32)
    lnv = sb("lnv", [128, 8], F32)
    lnscr = sb("lnscr", [128, D], F32)

    pz = [es.enter_context(nc.psum_tensor(f"pz_{i}", [128, 512], F32)) for i in range(2)]
    ptr = [es.enter_context(nc.psum_tensor(f"ptr_{i}", [128, 128], F32)) for i in range(2)]

    sIn = es.enter_context(nc.semaphore("sIn"))
    sPe = es.enter_context(nc.semaphore("sPe"))
    sAct = es.enter_context(nc.semaphore("sAct"))
    sDve = es.enter_context(nc.semaphore("sDve"))
    sOut = es.enter_context(nc.semaphore("sOut"))
    sVc = es.enter_context(nc.semaphore("sVc"))
    sAc = es.enter_context(nc.semaphore("sAc"))
    blk = es.enter_context(nc.Block())

    TH = TLOC // 2
    NTS_H = TH // 512
    PEH = 16 + 64
    ACH = 16 + 16
    DVH = 64 + 16

    def v_mm(h, g): return PEH * h + 5 * g + 1
    def v_tr(h, j): return PEH * h + 5 * (j // 4) + (j % 4) + 2
    def v_ev(h, g): return ACH * h + g + 1
    def v_sq(h, tt): return ACH * h + 16 + 2 * tt + 1
    def v_rstd(h, tt): return ACH * h + 16 + 2 * tt + 2
    def v_add(h, j): return DVH * h + j + 1
    def v_app(h, tt): return DVH * h + 64 + 2 * tt + 2

    @blk.sync
    def _(sy):
        for q in range(4):
            sy.dma_start(out=wcb[:, q * 8:(q + 1) * 8, :],
                         in_=wc_r[:, q * 8:(q + 1) * 8, :]).then_inc(sIn, 16)
        sy.dma_start(out=pb_s[:], in_=pb_i).then_inc(sIn, 16)
        sy.dma_start(out=eye_s[:], in_=eye_i).then_inc(sIn, 16)
        for h in range(2):
            sy.wait_ge(sIn, 96 + 80 * h)
            if h >= 1:
                sy.wait_ge(sDve, v_add(h - 1, 63))
                sy.wait_ge(sOut, 16 * h)
            for q in range(4):
                sy.dma_start(out=yh[:, q * 8:(q + 1) * 8, :],
                             in_=y_r[:, q * 8:(q + 1) * 8, h * TH:(h + 1) * TH]
                             ).then_inc(sIn, 16)
            sy.dma_start(out=xh[:], in_=x_r[:, h * 8:(h + 1) * 8, :]).then_inc(sIn, 16)

    @blk.tensor
    def _(tn):
        tn.wait_ge(sIn, 96)
        for h in range(2):
            tn.wait_ge(sIn, 96 + 80 * (h + 1))
            for oc in range(8):
                for ts in range(NTS_H):
                    g = oc * NTS_H + ts
                    G = 16 * h + g
                    if G >= 2:
                        hp, gp = (G - 2) // 16, (G - 2) % 16
                        tn.wait_ge(sAct, v_ev(hp, gp))
                    ins = None
                    for k in range(NK):
                        ins = tn.matmul(pz[g % 2][:], wcb[:, k, oc * 128:(oc + 1) * 128],
                                        yh[:, k, ts * 512:(ts + 1) * 512],
                                        start=(k == 0), stop=(k == NK - 1))
                    ins.then_inc(sPe, 1)
                    tn.wait_ge(sAct, v_ev(h, g))
                    for j in range(4):
                        jj = g * 4 + j
                        if 64 * h + jj >= 2:
                            hp = h if jj >= 2 else h - 1
                            jp = jj - 2 if jj >= 2 else 62 + jj
                            tn.wait_ge(sDve, v_add(hp, jp))
                        tn.transpose(ptr[jj % 2][:], scr[g % 2][:, j * 128:(j + 1) * 128],
                                     eye_s[:]).then_inc(sPe, 1)

    @blk.scalar
    def _(sc):
        sc.wait_ge(sIn, 96)
        for h in range(2):
            for g in range(16):
                oc = g // NTS_H
                G = 16 * h + g
                if G >= 2:
                    hp, gp = (G - 2) // 16, (G - 2) % 16
                    sc.wait_ge(sPe, v_tr(hp, gp * 4 + 3))
                sc.wait_ge(sPe, v_mm(h, g))
                sc.activation(out=scr[g % 2][:], in_=pz[g % 2][:], func=AF.Identity,
                              bias=pb_s[:, oc:oc + 1]).then_inc(sAct, 1)
            for tt in range(8):
                sc.wait_ge(sDve, DVH * h + 64)
                sc.activation(out=lnscr[:], in_=xmh[:, tt, :], func=AF.Square,
                              accum_out=sumsq[:, tt:tt + 1]).then_inc(sAct, 1)
                sc.wait_ge(sDve, DVH * h + 64 + 2 * tt + 1)
                sc.activation(out=lnv[:, tt:tt + 1], in_=var[:, tt:tt + 1], func=AF.Ln
                              ).then_inc(sAc, 1)
                sc.wait_ge(sAc, 8 * h + tt + 1)
                sc.activation(out=rstd[:, tt:tt + 1], in_=lnv[:, tt:tt + 1], func=AF.Exp,
                              scale=-0.5).then_inc(sAct, 1)

    @blk.vector
    def _(v):
        v.wait_ge(sIn, 96)
        v.memset(eps_s2[:], 1e-5)
        for h in range(2):
            if h >= 1:
                v.wait_ge(sOut, 16 * h)
            for g in range(16):
                oc, ts = g // NTS_H, g % NTS_H
                for j in range(4):
                    jj = g * 4 + j
                    tt = ts * 4 + j
                    v.wait_ge(sPe, v_tr(h, jj))
                    v.tensor_tensor(out=xmh[:, tt, oc * 128:(oc + 1) * 128],
                                    in0=ptr[jj % 2][:], in1=xh[:, tt, oc * 128:(oc + 1) * 128],
                                    op=Op.add).then_inc(sDve, 1)
            vb = 6 * 8 * h
            for tt in range(8):
                v.reduce_sum(mu[:, tt:tt + 1], xmh[:, tt, :], axis=mybir.AxisListType.X
                             ).then_inc(sVc, 1)
                v.wait_ge(sVc, vb + 6 * tt + 1)
                v.tensor_scalar(out=mu[:, tt:tt + 1], in0=mu[:, tt:tt + 1],
                                scalar1=1.0 / D, scalar2=None, op0=Op.mult, op1=Op.bypass
                                ).then_inc(sVc, 1)
                v.wait_ge(sVc, vb + 6 * tt + 2)
                v.tensor_scalar(out=mu2[:, tt:tt + 1], in0=mu[:, tt:tt + 1],
                                scalar1=mu[:, tt:tt + 1], scalar2=None,
                                op0=Op.mult, op1=Op.bypass).then_inc(sVc, 1)
                v.wait_ge(sVc, vb + 6 * tt + 3)
                v.wait_ge(sAct, v_sq(h, tt))
                v.scalar_tensor_tensor(out=var[:, tt:tt + 1], in0=sumsq[:, tt:tt + 1],
                                       scalar=1.0 / D, in1=mu2[:, tt:tt + 1],
                                       op0=Op.mult, op1=Op.subtract).then_inc(sVc, 1)
                v.wait_ge(sVc, vb + 6 * tt + 4)
                v.tensor_scalar(out=var[:, tt:tt + 1], in0=var[:, tt:tt + 1],
                                scalar1=eps_s2[:, 0:1], scalar2=None,
                                op0=Op.add, op1=Op.bypass).then_inc(sDve, 1)
                v.tensor_scalar(out=negmu[:, tt:tt + 1], in0=mu[:, tt:tt + 1],
                                scalar1=-1.0, scalar2=None, op0=Op.mult, op1=Op.bypass
                                ).then_inc(sVc, 1)
                v.wait_ge(sVc, vb + 6 * tt + 5)
                v.wait_ge(sAct, v_rstd(h, tt))
                v.tensor_scalar(out=xmh[:, tt, :], in0=xmh[:, tt, :],
                                scalar1=negmu[:, tt:tt + 1], scalar2=rstd[:, tt:tt + 1],
                                op0=Op.add, op1=Op.mult).then_inc(sVc, 1)
                v.wait_ge(sVc, vb + 6 * tt + 6)
                v.nop().then_inc(sDve, 1)

    @blk.gpsimd
    def _(g):
        for h in range(2):
            g.wait_ge(sOut, 16 * h)
            g.wait_ge(sDve, DVH * h + DVH)
            g.dma_start(out=o_r[:, h * 8:(h + 1) * 8, :], in_=xmh[:]).then_inc(sOut, 16)
        g.wait_ge(sOut, 32)

    es.close()
    return nc


def host_L3_maps(inp, r1, r2):
    x2d = np.asarray(inp["L0_attn_x"], np.float32)[0]
    proj_b = np.asarray(inp["L0_attn_proj_b"], np.float32)
    wc_all = cc(np.concatenate([np.asarray(r1.results[c]["wc"]) for c in range(NCORES)],
                               axis=0))
    ys = [np.asarray(r2.results[c]["y"]) for c in range(NCORES)]
    eye = np.eye(128, dtype=np.float32)
    pb = cc(proj_b.reshape(8, 128).T)
    maps = []
    for j in range(NCORES):
        sl = slice(j * TLOC, (j + 1) * TLOC)
        fwd = [ys[c][:, sl] for c in range(4)]
        bwd = [ys[c][:, ::-1][:, sl] for c in range(4, 8)]
        y_all = cc(np.concatenate(fwd + bwd, axis=0))
        maps.append(dict(
            y_all=y_all, wc_all=wc_all, x_sl=cc(x2d[sl]), pb=pb, eyef=eye,
        ))
    return maps





def build_L4():
    nc = bass.Bass("TRN2", target_bir_lowering=False)
    xm_i = nc.dram_tensor("x_mid", [TLOC, D], F32, kind="ExternalInput").ap()
    ngrep_i = nc.dram_tensor("ng_rep", [128, D], F32, kind="ExternalInput").ap()
    ngc_i = nc.dram_tensor("ng_cols", [128, 8], F32, kind="ExternalInput").ap()
    sgc_i = nc.dram_tensor("sg_cols", [128, 8], F32, kind="ExternalInput").ap()
    se_i = nc.dram_tensor("se", [E * S, D], F32, kind="ExternalInput").ap()
    seT_i = nc.dram_tensor("seT", [D, E * S], F32, kind="ExternalInput").ap()
    eyef_i = nc.dram_tensor("eyef", [128, 128], F32, kind="ExternalInput").ap()
    eyeb_i = nc.dram_tensor("eyeb", [128, 128], BF16, kind="ExternalInput").ap()
    o_ps = nc.dram_tensor("pslots", [E * S, D], F32, kind="ExternalOutput").ap()
    o_cm = nc.dram_tensor("cm", [128, 32], F32, kind="ExternalOutput").ap()
    o_cs = nc.dram_tensor("cs", [128, 32], F32, kind="ExternalOutput").ap()
    snT_d = nc.dram_tensor("snT_d", [D, E * S], BF16)

    MS = (E * S) // 128
    NTT = TLOC // 128
    xm_r = xm_i.rearrange("(a p) c -> p a c", p=128)
    se_r = se_i.rearrange("(a p) c -> p a c", p=128)
    seT_r = seT_i.rearrange("(a p) c -> p a c", p=128)
    snT_w = snT_d.rearrange("(a p) c -> p a c", p=128)
    snT_b = snT_d.rearrange("(a p) (mt m) -> p a mt m", p=128, m=128)
    ps_r = o_ps.rearrange("(a p) c -> p a c", p=128)

    es = ExitStack()
    sb = lambda name, shape, dt: es.enter_context(nc.sbuf_tensor(name, shape, dt))

    xm = sb("xm", [128, NTT, D], F32)
    xnb = sb("xnb", [128, NTT, D], BF16)
    xnT = sb("xnT", [128, 8, TLOC], BF16)
    se_scr = [sb(f"se_scr{i}", [128, D], F32) for i in range(2)]
    seT_scr = [sb("seT_scr0", [128, E * S], F32)]
    sntb = sb("sntb", [128, E * S], BF16)
    snt_scr = [sb(f"snt_scr{i}", [128, 8, 128], BF16) for i in range(2)]
    sq_scr = sb("sq_scr", [128, D], F32)
    ssq = sb("ssq", [128, MS], F32)
    s_all = sb("s_all", [128, MS], F32)
    scr1 = sb("scr1", [128, MS], F32)
    sxq = sb("sxq", [128, NTT], F32)
    sx = sb("sx", [128, NTT], F32)
    cm_all = sb("cm_all", [128, MS], F32)
    ncm_all = sb("ncm_all", [128, MS], F32)
    cs_all = sb("cs_all", [128, MS], F32)
    Lt = [sb(f"Lt{i}", [128, TLOC], F32) for i in range(2)]
    Eb = [sb(f"Eb{i}", [128, TLOC], BF16) for i in range(2)]
    Et = [sb(f"Et{i}", [128, NTT, 128], BF16) for i in range(2)]
    ngrep = sb("ngrep", [128, D], F32)
    ngc = sb("ngc", [128, 8], F32)
    sgc = sb("sgc", [128, 8], F32)
    eyef = sb("eyef_s", [128, 128], F32)
    eyeb = sb("eyeb_s", [128, 128], BF16)

    pL = [es.enter_context(nc.psum_tensor(f"pL{i}", [128, 512], F32)) for i in range(2)]
    pS = [es.enter_context(nc.psum_tensor(f"pS{i}", [128, 512], F32)) for i in range(2)]
    ptrf = es.enter_context(nc.psum_tensor("ptrf", [128, 128], F32))
    ptrb = [es.enter_context(nc.psum_tensor(f"ptrb{i}", [128, 128], BF16)) for i in range(2)]

    sem = lambda n: es.enter_context(nc.semaphore(n))
    sIn, sSe, sSeT, sSnt = sem("sIn"), sem("sSe"), sem("sSeT"), sem("sSnt")
    sAsq, sAxn, sAev, sAex, sAps = sem("sAsq"), sem("sAxn"), sem("sAev"), sem("sAex"), sem("sAps")
    sAc = sem("sAc")
    sDsn, sDxn, sDcm, sDcs, sEtc = sem("sDsn"), sem("sDxn"), sem("sDcm"), sem("sDcs"), sem("sEtc")
    sVc = sem("sVc")
    sPeT, sPeL, sPeE, sPeS = sem("sPeT"), sem("sPeL"), sem("sPeE"), sem("sPeS")
    sOsn, sOps = sem("sOsn"), sem("sOps")
    blk = es.enter_context(nc.Block())

    vc_ssq = 1
    vc_sxq = 2
    def vc_sg(j): return 2 + 2 * j + 1
    def vc_snb(j): return 2 + 2 * j + 2
    VC0 = 2 + 16
    def vc_cmax(m): return VC0 + m + 1

    @blk.sync
    def _(sy):
        sy.dma_start(out=xm[:, 0:8, :], in_=xm_r[:, 0:8, :]).then_inc(sIn, 16)
        sy.dma_start(out=xm[:, 8:16, :], in_=xm_r[:, 8:16, :]).then_inc(sIn, 16)
        sy.dma_start(out=ngrep[:], in_=ngrep_i).then_inc(sIn, 16)
        sy.dma_start(out=ngc[:], in_=ngc_i).then_inc(sIn, 16)
        sy.dma_start(out=sgc[:], in_=sgc_i).then_inc(sIn, 16)
        sy.dma_start(out=eyef[:], in_=eyef_i).then_inc(sIn, 16)
        sy.dma_start(out=eyeb[:], in_=eyeb_i).then_inc(sIn, 16)
        for m in range(MS):
            sy.wait_ge(sSe, 16 * m)
            if m >= 2:
                sy.wait_ge(sAsq, m - 1)
            sy.dma_start(out=se_scr[m % 2][:], in_=se_r[:, m, :]).then_inc(sSe, 16)
        for j in range(8):
            sy.wait_ge(sSeT, 16 * j)
            if j >= 1:
                sy.wait_ge(sVc, vc_snb(j - 1))
            sy.dma_start(out=seT_scr[0][:], in_=seT_r[:, j, :]).then_inc(sSeT, 16)
        sy.wait_ge(sOsn, 16 * 8)
        for m in range(MS):
            sy.wait_ge(sSnt, 16 * m)
            if m >= 2:
                sy.wait_ge(sPeL, 32 * (m - 1))
            sy.dma_start(out=snt_scr[m % 2][:], in_=snT_b[:, :, m, :]).then_inc(sSnt, 16)

    @blk.scalar
    def _(sc):
        sc.wait_ge(sIn, 112)
        for m in range(MS):
            sc.wait_ge(sSe, 16 * (m + 1))
            sc.activation(out=sq_scr[:], in_=se_scr[m % 2][:], func=AF.Square,
                          accum_out=ssq[:, m:m + 1]).then_inc(sAsq, 1)
        for tt in range(NTT):
            sc.activation(out=sq_scr[:], in_=xm[:, tt, :], func=AF.Square,
                          accum_out=sxq[:, tt:tt + 1]).then_inc(sAsq, 1)
        sc.wait_ge(sVc, vc_ssq)
        sc.activation(out=scr1[:], in_=ssq[:], func=AF.Ln).then_inc(sAc, 1)
        sc.wait_ge(sAc, 1)
        sc.activation(out=s_all[:], in_=scr1[:], func=AF.Exp, scale=-0.5).then_inc(sAc, 1)
        sc.wait_ge(sVc, vc_sxq)
        sc.activation(out=sxq[:], in_=sxq[:], func=AF.Ln).then_inc(sAc, 1)
        sc.wait_ge(sAc, 3)
        sc.activation(out=sx[:], in_=sxq[:], func=AF.Exp, scale=-0.5).then_inc(sAc, 1)
        for j in range(8):
            for tt in range(NTT):
                k = j * NTT + tt
                sc.wait_ge(sPeT, k + 1)
                sc.activation(out=xnT[:, j, tt * 128:(tt + 1) * 128], in_=ptrf[:],
                              func=AF.Identity, scale=ngc[:, j:j + 1]).then_inc(sAxn, 1)
        for m in range(MS):
            for ts in range(4):
                if m >= 2:
                    sc.wait_ge(sVc, vc_cmax(m - 2))
                    sc.wait_ge(sOps, 16 * (2 * (m - 2) + 2))
                sc.wait_ge(sPeL, 32 * m + 8 * (ts + 1))
                sc.activation(out=Lt[m % 2][:, ts * 512:(ts + 1) * 512],
                              in_=pL[(4 * m + ts) % 2][:], func=AF.Identity,
                              scale=s_all[:, m:m + 1]).then_inc(sAev, 1)
            sc.wait_ge(sDcm, m + 1)
            if m >= 2:
                sc.wait_ge(sDcs, m - 1)
                sc.wait_ge(sPeE, 16 * (m - 1))
            sc.activation(out=Eb[m % 2][:], in_=Lt[m % 2][:], func=AF.Exp,
                          bias=ncm_all[:, m:m + 1]).then_inc(sAex, 1)
            for dc in range(2):
                sc.wait_ge(sPeS, 2 * m + dc + 1)
                sc.activation(out=Lt[m % 2][:, dc * 512:(dc + 1) * 512],
                              in_=pS[dc][:], func=AF.Copy).then_inc(sAps, 1)

    @blk.vector
    def _(v):
        v.wait_ge(sIn, 112)
        v.wait_ge(sAsq, MS)
        v.tensor_scalar(out=ssq[:], in0=ssq[:], scalar1=1e-24, scalar2=None,
                        op0=Op.max, op1=Op.bypass).then_inc(sVc, 1)
        v.wait_ge(sAsq, MS + NTT)
        v.tensor_scalar(out=sxq[:], in0=sxq[:], scalar1=1e-24, scalar2=None,
                        op0=Op.max, op1=Op.bypass).then_inc(sVc, 1)
        for j in range(8):
            v.wait_ge(sSeT, 16 * (j + 1))
            v.tensor_scalar(out=seT_scr[0][:], in0=seT_scr[0][:],
                            scalar1=sgc[:, j:j + 1], scalar2=None,
                            op0=Op.mult, op1=Op.bypass).then_inc(sVc, 1)
            v.wait_ge(sVc, vc_sg(j))
            if j >= 1:
                v.wait_ge(sOsn, 16 * j)
            v.tensor_copy(sntb[:], seT_scr[0][:]).then_inc(sVc, 1)
        v.wait_ge(sAc, 4)
        for tt in range(NTT):
            v.tensor_scalar(out=xm[:, tt, :], in0=xm[:, tt, :],
                            scalar1=sx[:, tt:tt + 1], scalar2=None,
                            op0=Op.mult, op1=Op.bypass).then_inc(sDxn, 1)
        for tt in range(NTT):
            v.tensor_tensor(out=xnb[:, tt, :], in0=xm[:, tt, :], in1=ngrep[:],
                            op=Op.mult).then_inc(sDxn, 1)
        for m in range(MS):
            v.wait_ge(sAev, 4 * (m + 1))
            v.reduce_max(cm_all[:, m:m + 1], Lt[m % 2][:], axis=mybir.AxisListType.X
                         ).then_inc(sVc, 1)
            v.wait_ge(sVc, vc_cmax(m))
            v.tensor_scalar(out=ncm_all[:, m:m + 1], in0=cm_all[:, m:m + 1],
                            scalar1=-1.0, scalar2=None, op0=Op.mult, op1=Op.bypass
                            ).then_inc(sDcm, 1)
            v.wait_ge(sAex, m + 1)
            v.reduce_sum(cs_all[:, m:m + 1], Eb[m % 2][:], axis=mybir.AxisListType.X
                         ).then_inc(sDcs, 1)
            for tt in range(NTT):
                k = m * NTT + tt
                v.wait_ge(sPeE, k + 1)
                if m >= 2 and tt == 0:
                    v.wait_ge(sPeS, 2 * (m - 2) + 2)
                v.tensor_copy(Et[m % 2][:, tt, :], ptrb[k % 2][:]).then_inc(sEtc, 1)

    @blk.tensor
    def _(tn):
        tn.wait_ge(sIn, 112)
        tn.wait_ge(sDxn, NTT)
        for j in range(8):
            for tt in range(NTT):
                k = j * NTT + tt
                if k >= 1:
                    tn.wait_ge(sAxn, k)
                tn.transpose(ptrf[:], xm[:, tt, j * 128:(j + 1) * 128], eyef[:]
                             ).then_inc(sPeT, 1)
        tn.wait_ge(sDxn, 2 * NTT)
        for m in range(MS):
            tn.wait_ge(sSnt, 16 * (m + 1))
            for ts in range(4):
                g = 4 * m + ts
                if g >= 2:
                    tn.wait_ge(sAev, g - 1)
                ins = None
                for k in range(8):
                    ins = tn.matmul(pL[g % 2][:], snt_scr[m % 2][:, k, :],
                                    xnT[:, k, ts * 512:(ts + 1) * 512],
                                    start=(k == 0), stop=(k == 7))
                ins.then_inc(sPeL, 8)
            tn.wait_ge(sAex, m + 1)
            for tt in range(NTT):
                k = m * NTT + tt
                if k >= 2:
                    tn.wait_ge(sEtc, k - 1)
                tn.transpose(ptrb[k % 2][:], Eb[m % 2][:, tt * 128:(tt + 1) * 128],
                             eyeb[:]).then_inc(sPeE, 1)
            tn.wait_ge(sEtc, 16 * (m + 1))
            for dc in range(2):
                if 2 * m + dc >= 2:
                    tn.wait_ge(sAps, 2 * m + dc - 1)
                ins = None
                for tt in range(NTT):
                    ins = tn.matmul(pS[dc][:], Et[m % 2][:, tt, :],
                                    xnb[:, tt, dc * 512:(dc + 1) * 512],
                                    start=(tt == 0), stop=(tt == NTT - 1))
                ins.then_inc(sPeS, 1)

    @blk.gpsimd
    def _(g):
        for j in range(8):
            g.wait_ge(sOsn, 16 * j)
            g.wait_ge(sVc, vc_snb(j))
            g.dma_start(out=snT_w[:, j, :], in_=sntb[:]).then_inc(sOsn, 16)
        for m in range(MS):
            for dc in range(2):
                g.wait_ge(sOps, 16 * (2 * m + dc))
                g.wait_ge(sAps, 2 * m + dc + 1)
                g.dma_start(out=ps_r[:, m, dc * 512:(dc + 1) * 512],
                            in_=Lt[m % 2][:, dc * 512:(dc + 1) * 512]).then_inc(sOps, 16)
        g.wait_ge(sDcm, MS)
        g.wait_ge(sDcs, MS)
        g.dma_start(out=o_cm, in_=cm_all[:]).then_inc(sOps, 16)
        g.dma_start(out=o_cs, in_=cs_all[:]).then_inc(sOps, 16)
        g.wait_ge(sOps, 16 * (2 * MS + 2))
        g.wait_ge(sOsn, 16 * 8)

    es.close()
    return nc


def host_L4_maps(inp, r3):
    ng = np.asarray(inp["L0_ffn_norm_gamma"], np.float32)
    sg = np.asarray(inp["L0_ffn_slot_norm_gamma"], np.float32)
    se = cc(np.asarray(inp["L0_ffn_slot_embeds"], np.float32).reshape(E * S, D))
    eyef = np.eye(128, dtype=np.float32)
    eyeb = np.eye(128).astype(BF)
    ngs = ng * SCALE
    sgs = sg * SCALE
    maps = []
    for j in range(NCORES):
        maps.append(dict(
            x_mid=np.asarray(r3.results[j]["x_mid"]),
            ng_rep=cc(np.broadcast_to(ngs, (128, D))),
            ng_cols=cc(ngs.reshape(8, 128).T),
            sg_cols=cc(sgs.reshape(8, 128).T),
            se=se, seT=cc(se.T), eyef=eyef, eyeb=eyeb,
        ))
    return maps





def build_L5():
    nc = bass.Bass("TRN2", target_bir_lowering=False)
    parts_i = nc.dram_tensor("parts", [NCORES, S, D], F32, kind="ExternalInput").ap()
    cm8_i = nc.dram_tensor("cm8", [NCORES, 128, 4], F32, kind="ExternalInput").ap()
    cs8_i = nc.dram_tensor("cs8", [NCORES, 128, 4], F32, kind="ExternalInput").ap()
    w1_i = nc.dram_tensor("w1e", [D, HID], F32, kind="ExternalInput").ap()
    w2_i = nc.dram_tensor("w2e", [HID, D], F32, kind="ExternalInput").ap()
    b1_i = nc.dram_tensor("b1c", [128, 32], F32, kind="ExternalInput").ap()
    b2_i = nc.dram_tensor("b2c", [128, 8], F32, kind="ExternalInput").ap()
    eyef_i = nc.dram_tensor("eyef", [128, 128], F32, kind="ExternalInput").ap()
    o_eoT = nc.dram_tensor("eoT", [D, S], F32, kind="ExternalOutput").ap()

    parts_r = parts_i.rearrange("c (a p) d -> p c a d", p=128)
    w1_r = w1_i.rearrange("(a p) h -> p a h", p=128)
    w2_r = w2_i.rearrange("(a p) d -> p a d", p=128)
    eoT_r = o_eoT.rearrange("(a p) s -> p a s", p=128)

    es = ExitStack()
    sb = lambda name, shape, dt: es.enter_context(nc.sbuf_tensor(name, shape, dt))
    pscr = [sb(f"pscr{i}", [128, 4, D], F32) for i in range(2)]
    acc = sb("acc", [128, 4, D], F32)
    cmt = sb("cmt", [128, 32], F32)
    cst = sb("cst", [128, 32], F32)
    beta = sb("beta", [128, 4], F32)
    brep = sb("brep", [128, 32], F32)
    sdiff = sb("sdiff", [128, 32], F32)
    scl = sb("scl", [128, 32], F32)
    csg = sb("csg", [128, 4], F32)
    cscr = sb("cscr", [128, 4], F32)
    rcs = sb("rcs", [128, 4], F32)
    slotsT = sb("slotsT", [128, 8, S], BF16)
    wscr = [sb(f"wscr{i}", [128, HID], F32) for i in range(2)]
    wbk = [sb(f"wbk{i}", [128, HID], BF16) for i in range(2)]
    hT = sb("hT", [128, 32, S], BF16)
    eoT_s = sb("eoT_s", [128, 8, S], F32)
    b1c = sb("b1c_s", [128, 32], F32)
    b2c = sb("b2c_s", [128, 8], F32)
    eyef_s = sb("eyef_s", [128, 128], F32)

    p8 = [es.enter_context(nc.psum_tensor(f"p8_{i}", [128, 512], F32)) for i in range(8)]

    sem = lambda n: es.enter_context(nc.semaphore(n))
    sIn = sem("sIn"); sW = sem("sW"); sVc = sem("sVc"); sAc = sem("sAc")
    sDs1 = sem("sDs1"); sVa = sem("sVa"); sDsl = sem("sDsl"); sEv = sem("sEv")
    sDw = sem("sDw"); sAg = sem("sAg"); sAe = sem("sAe"); sPe = sem("sPe")
    sOut = sem("sOut")
    blk = es.enter_context(nc.Block())

    @blk.sync
    def _(sy):
        sy.dma_start(out=cmt[:].rearrange("p (c a) -> p c a", a=4),
                     in_=cm8_i.rearrange("c p a -> p c a")).then_inc(sIn, 16)
        sy.dma_start(out=cst[:].rearrange("p (c a) -> p c a", a=4),
                     in_=cs8_i.rearrange("c p a -> p c a")).then_inc(sIn, 16)
        sy.dma_start(out=b1c[:], in_=b1_i).then_inc(sIn, 16)
        sy.dma_start(out=b2c[:], in_=b2_i).then_inc(sIn, 16)
        sy.dma_start(out=eyef_s[:], in_=eyef_i).then_inc(sIn, 16)
        for c in range(NCORES):
            sy.wait_ge(sIn, 80 + 16 * c)
            if c >= 2:
                sy.wait_ge(sVa, c - 1)
            sy.dma_start(out=pscr[c % 2][:], in_=parts_r[:, c, :, :]).then_inc(sIn, 16)
        for q in range(4):
            for k in range(8):
                i = q * 8 + k
                sy.wait_ge(sW, 16 * i)
                if i >= 2:
                    sy.wait_ge(sDw, i - 1)
                sy.dma_start(out=wscr[i % 2][:], in_=w1_r[:, k, :]).then_inc(sW, 16)
        for k in range(32):
            i = 32 + k
            sy.wait_ge(sW, 16 * i)
            sy.wait_ge(sDw, i - 1)
            sy.dma_start(out=wscr[i % 2][:, 0:D], in_=w2_r[:, k, :]).then_inc(sW, 16)

    @blk.vector
    def _(v):
        vn = [0]
        def ch(instr):
            instr.then_inc(sVc, 1)
            vn[0] += 1
            v.wait_ge(sVc, vn[0])
        v.wait_ge(sIn, 80)
        v.tensor_tensor(out=beta[:], in0=cmt[:, 0:4], in1=cmt[:, 4:8], op=Op.max)
        for c in range(2, NCORES):
            ins = v.tensor_tensor(out=beta[:], in0=beta[:], in1=cmt[:, 4 * c:4 * c + 4],
                                  op=Op.max)
        ch(ins)
        for c in range(NCORES):
            ins = v.tensor_copy(brep[:, 4 * c:4 * c + 4], beta[:])
        ch(ins)
        ins = v.tensor_tensor(out=sdiff[:], in0=cmt[:], in1=brep[:], op=Op.subtract)
        ch(ins)
        v.nop().then_inc(sDs1, 1)
        v.wait_ge(sAc, 1)
        ins = v.tensor_tensor(out=csg[:], in0=cst[:, 0:4], in1=scl[:, 0:4], op=Op.mult)
        ch(ins)
        for c in range(1, NCORES):
            ins = v.tensor_tensor(out=cscr[:], in0=cst[:, 4 * c:4 * c + 4],
                                  in1=scl[:, 4 * c:4 * c + 4], op=Op.mult)
            ch(ins)
            ins = v.tensor_tensor(out=csg[:], in0=csg[:], in1=cscr[:], op=Op.add)
            ch(ins)
        ins = v.reciprocal(rcs[:], csg[:])
        ch(ins)
        for c in range(NCORES):
            v.wait_ge(sIn, 80 + 16 * (c + 1))
            for st in range(4):
                if c == 0:
                    ins = v.tensor_scalar(out=acc[:, st, :], in0=pscr[0][:, st, :],
                                          scalar1=scl[:, st:st + 1], scalar2=None,
                                          op0=Op.mult, op1=Op.bypass)
                else:
                    ins = v.scalar_tensor_tensor(out=acc[:, st, :],
                                                 in0=pscr[c % 2][:, st, :],
                                                 scalar=scl[:, 4 * c + st:4 * c + st + 1],
                                                 in1=acc[:, st, :], op0=Op.mult, op1=Op.add)
            ins.then_inc(sVa, 1)
        v.wait_ge(sVa, NCORES)
        for st in range(4):
            ins = v.tensor_scalar(out=acc[:, st, :], in0=acc[:, st, :],
                                  scalar1=rcs[:, st:st + 1], scalar2=None,
                                  op0=Op.mult, op1=Op.bypass)
        ch(ins)
        v.nop().then_inc(sDsl, 1)
        for j in range(32):
            st, dj = j // 8, j % 8
            v.wait_ge(sPe, j + 1)
            v.tensor_copy(slotsT[:, dj, st * 128:(st + 1) * 128], p8[j % 2][:, 0:128]
                          ).then_inc(sEv, 1)
        for i in range(64):
            v.wait_ge(sW, 16 * (i + 1))
            if i >= 2:
                if i < 32 or i >= 34:
                    pass
            if i < 32:
                q, k = i // 8, i % 8
                if i >= 2:
                    v.wait_ge(sPe, 32 + (i - 2) + 1)
                v.tensor_copy(wbk[i % 2][:], wscr[i % 2][:]).then_inc(sDw, 1)
            else:
                k = i - 32
                if i >= 34:
                    v.wait_ge(sPe, 64 + (i - 34) + 1)
                elif i >= 2:
                    v.wait_ge(sPe, 64)
                v.tensor_copy(wbk[i % 2][:, 0:D], wscr[i % 2][:, 0:D]).then_inc(sDw, 1)

    @blk.scalar
    def _(sc):
        sc.wait_ge(sDs1, 1)
        sc.activation(out=scl[:], in_=sdiff[:], func=AF.Exp).then_inc(sAc, 1)
        for q in range(4):
            for hc8 in range(8):
                hc = q * 8 + hc8
                sc.wait_ge(sPe, 32 + q * 8 + 8)
                sc.activation(out=hT[:, hc, :], in_=p8[hc8][:], func=AF.Gelu,
                              bias=b1c[:, hc:hc + 1]).then_inc(sAg, 1)
        for dj in range(8):
            sc.wait_ge(sPe, 96)
            sc.activation(out=eoT_s[:, dj, :], in_=p8[dj][:], func=AF.Identity,
                          bias=b2c[:, dj:dj + 1]).then_inc(sAe, 1)

    @blk.tensor
    def _(tn):
        tn.wait_ge(sDsl, 1)
        for j in range(32):
            st, dj = j // 8, j % 8
            if j >= 2:
                tn.wait_ge(sEv, j - 1)
            tn.transpose(p8[j % 2][:, 0:128], acc[:, st, dj * 128:(dj + 1) * 128],
                         eyef_s[:]).then_inc(sPe, 1)
        for q in range(4):
            for k in range(8):
                i = q * 8 + k
                tn.wait_ge(sDw, i + 1)
                tn.wait_ge(sEv, 32)
                if q >= 1:
                    tn.wait_ge(sAg, 8 * (q - 1) + 8)
                ins = None
                for hc8 in range(8):
                    hc = q * 8 + hc8
                    ins = tn.matmul(p8[hc8][:], wbk[i % 2][:, hc * 128:(hc + 1) * 128],
                                    slotsT[:, k, :], start=(k == 0), stop=(k == 7))
                ins.then_inc(sPe, 1)
        for k in range(32):
            i = 32 + k
            tn.wait_ge(sDw, i + 1)
            tn.wait_ge(sAg, k + 1)
            if k == 0:
                tn.wait_ge(sAg, 32)
            ins = None
            for dj in range(8):
                ins = tn.matmul(p8[dj][:], wbk[i % 2][:, dj * 128:(dj + 1) * 128],
                                hT[:, k, :], start=(k == 0), stop=(k == 31))
            ins.then_inc(sPe, 1)

    @blk.gpsimd
    def _(g):
        g.wait_ge(sAe, 8)
        g.dma_start(out=eoT_r[:], in_=eoT_s[:]).then_inc(sOut, 16)
        g.wait_ge(sOut, 16)

    es.close()
    return nc


def host_L5_maps(inp, r4):
    w1 = np.asarray(inp["L0_ffn_w1"], np.float32)
    w2 = np.asarray(inp["L0_ffn_w2"], np.float32)
    b1 = np.asarray(inp["L0_ffn_b1"], np.float32)
    b2 = np.asarray(inp["L0_ffn_b2"], np.float32)
    maps = []
    for e in range(NCORES):
        sl = slice(e * S, (e + 1) * S)
        maps.append(dict(
            parts=cc(np.stack([np.asarray(r4.results[c]["pslots"])[sl] for c in range(NCORES)])),
            cm8=cc(np.stack([np.asarray(r4.results[c]["cm"])[:, 4 * e:4 * e + 4]
                             for c in range(NCORES)])),
            cs8=cc(np.stack([np.asarray(r4.results[c]["cs"])[:, 4 * e:4 * e + 4]
                             for c in range(NCORES)])),
            w1e=cc(w1[e]), w2e=cc(w2[e]),
            b1c=cc(b1[e].reshape(32, 128).T),
            b2c=cc(b2[e].reshape(8, 128).T),
            eyef=np.eye(128, dtype=np.float32),
        ))
    return maps





def build_L6():
    nc = bass.Bass("TRN2", target_bir_lowering=False)
    xm_i = nc.dram_tensor("x_mid", [TLOC, D], F32, kind="ExternalInput").ap()
    ngc_i = nc.dram_tensor("ng_cols", [128, 8], F32, kind="ExternalInput").ap()
    sgc_i = nc.dram_tensor("sg_cols", [128, 8], F32, kind="ExternalInput").ap()
    se_i = nc.dram_tensor("se", [E * S, D], F32, kind="ExternalInput").ap()
    seT_i = nc.dram_tensor("seT", [D, E * S], F32, kind="ExternalInput").ap()
    eo_i = nc.dram_tensor("eo_all", [E * S, D], F32, kind="ExternalInput").ap()
    eyef_i = nc.dram_tensor("eyef", [128, 128], F32, kind="ExternalInput").ap()
    onesc_i = nc.dram_tensor("onesc", [128, 1], BF16, kind="ExternalInput").ap()
    o_out = nc.dram_tensor("out", [TLOC, D], F32, kind="ExternalOutput").ap()
    snT_d = nc.dram_tensor("snT_d", [D, E * S], BF16)
    Ed = nc.dram_tensor("Ed", [E * S, TLOC], BF16)
    Ebo_d = nc.dram_tensor("Ebo_d", [E * S, D], BF16)
    al_d = nc.dram_tensor("al_d", [TLOC], BF16)
    den_d = nc.dram_tensor("den_d", [TLOC], F32)
    s_d = nc.dram_tensor("s_d", [E * S], F32)

    MS = (E * S) // 128
    NTT = TLOC // 128
    xm_r = xm_i.rearrange("(a p) c -> p a c", p=128)
    se_r = se_i.rearrange("(a p) c -> p a c", p=128)
    seT_r = seT_i.rearrange("(a p) c -> p a c", p=128)
    eo_r = eo_i.rearrange("(a p) c -> p a c", p=128)
    snT_w = snT_d.rearrange("(a p) c -> p a c", p=128)
    snT_b = snT_d.rearrange("(a p) (mt m) -> p a mt m", p=128, m=128)
    snT_c = snT_d.rearrange("(a p) (mc m) -> p a mc m", p=128, m=512)
    Ed_r = Ed.rearrange("(a p) t -> p a t", p=128)
    Ebo_r = Ebo_d.rearrange("(a p) c -> p a c", p=128)
    al_w = al_d.rearrange("(a p) -> p a", p=128)
    den_wT = den_d.rearrange("(a p) -> p a", p=128)
    o_r = o_out.rearrange("(a p) c -> p a c", p=128)

    es = ExitStack()
    sb = lambda name, shape, dt: es.enter_context(nc.sbuf_tensor(name, shape, dt))
    xm = sb("xm6", [128, NTT, D], F32)
    xnT = sb("xnT6", [128, 8, TLOC], BF16)
    se_scr = [sb(f"se6_{i}", [128, D], F32) for i in range(2)]
    seT_scr = sb("seT6", [128, E * S // 2], F32)
    sntb = sb("sntb6", [128, E * S // 2], BF16)
    snt_scr = [sb(f"snt6_{i}", [128, 8, 128], BF16) for i in range(2)]
    sntc_scr = [sb("sntc6_0", [128, 8, 512], BF16)]
    eo_scr = [sb(f"eo6_{i}", [128, D], F32) for i in range(2)]
    eob = sb("eob6", [128, D], BF16)
    eob2 = [sb(f"eob26_{i}", [128, D], BF16) for i in range(2)]
    Ebl = [sb(f"Ebl6_{i}", [128, 512], BF16) for i in range(2)]
    sq_scr = sb("sq6", [128, D], F32)
    ssq = sb("ssq6", [128, MS], F32)
    s_all = sb("sall6", [128, MS], F32)
    scr1 = sb("scr16", [128, MS], F32)
    sxq = sb("sxq6", [128, NTT], F32)
    sx = sb("sx6", [128, NTT], F32)
    alpha = sb("alpha6", [128, NTT], F32)
    arep = sb("arep6", [128, TLOC], BF16)
    asc = sb("asc6", [128, 512], F32)
    schk = sb("schk6", [128, 512], F32)
    Lt = [sb(f"Lt6_{i}", [128, TLOC], F32) for i in range(2)]
    Etb = sb("Etb6", [128, TLOC], BF16)
    den = sb("den6", [1, TLOC], F32)
    denT = sb("denT6", [128, NTT], F32)
    rden = sb("rden6", [128, NTT], F32)
    xo = sb("xo6", [128, D], F32)
    xres = [sb(f"xres6_{i}", [128, D], F32) for i in range(2)]
    ngc = sb("ngc6", [128, 8], F32)
    sgc = sb("sgc6", [128, 8], F32)
    eyef_s = sb("eyef6", [128, 128], F32)
    onesc = sb("onesc6", [128, 1], BF16)
    mu = sb("mu6", [128, NTT], F32)
    mu2 = sb("mu26", [128, NTT], F32)
    var = sb("var6", [128, NTT], F32)
    lnv = sb("lnv6", [128, NTT], F32)
    rstd = sb("rstd6", [128, NTT], F32)
    negmu = sb("negmu6", [128, NTT], F32)
    sumsq = sb("sumsq6", [128, NTT], F32)
    eps_s = sb("eps6", [128, 1], F32)

    p8 = [es.enter_context(nc.psum_tensor(f"p86_{i}", [128, 512], F32)) for i in range(8)]

    sem = lambda n: es.enter_context(nc.semaphore(n))
    sIn, sSe, sSeT, sEo, sSntc, sSnt = (sem("sIn"), sem("sSe"), sem("sSeT"),
                                        sem("sEo"), sem("sSntc"), sem("sSnt"))
    sIn2, sIn3, sIn4, sEdi, sEoi = sem("sIn2"), sem("sIn3"), sem("sIn4"), sem("sEdi"), sem("sEoi")
    sAsq, sAc, sAx, sAev, sAex, sAo = (sem("sAsq"), sem("sAc"), sem("sAx"),
                                       sem("sAev"), sem("sAex"), sem("sAo"))
    sVc, sVs, sDeb, sDxn, sDal, sDet, sDdn, sVl = (sem("sVc"), sem("sVs"), sem("sDeb"),
                                                   sem("sDxn"), sem("sDal"), sem("sDet"),
                                                   sem("sDdn"), sem("sVl"))
    sPeT, sPeA, sPeL, sPeDn, sPeC, sPeG = (sem("sPeT"), sem("sPeA"), sem("sPeL"),
                                           sem("sPeDn"), sem("sPeC"), sem("sPeG"))
    sOsn, sOB, sOA, sOE, sODn, sOX = (sem("sOsn"), sem("sOB"), sem("sOA"),
                                      sem("sOE"), sem("sODn"), sem("sOX"))
    sVg = sem("sVg"); sScd = sem("sScd"); sDsc = sem("sDsc"); sOsd = sem("sOsd")
    blk = es.enter_context(nc.Block())

    @blk.sync
    def _(sy):
        sy.dma_start(out=xm[:, 0:8, :], in_=xm_r[:, 0:8, :]).then_inc(sIn, 16)
        sy.dma_start(out=xm[:, 8:16, :], in_=xm_r[:, 8:16, :]).then_inc(sIn, 16)
        sy.dma_start(out=ngc[:], in_=ngc_i).then_inc(sIn, 16)
        sy.dma_start(out=sgc[:], in_=sgc_i).then_inc(sIn, 16)
        sy.dma_start(out=eyef_s[:], in_=eyef_i).then_inc(sIn, 16)
        sy.dma_start(out=onesc[:], in_=onesc_i).then_inc(sIn, 16)
        for m in range(MS):
            sy.wait_ge(sSe, 16 * m)
            if m >= 2:
                sy.wait_ge(sAsq, m - 1)
            sy.dma_start(out=se_scr[m % 2][:], in_=se_r[:, m, :]).then_inc(sSe, 16)
        for jh in range(16):
            j, hf = jh // 2, jh % 2
            sy.wait_ge(sSeT, 16 * jh)
            if jh >= 1:
                sy.wait_ge(sVc, 2 + 2 * (jh - 1) + 2)
            sy.dma_start(out=seT_scr[:],
                         in_=seT_r[:, j, hf * (E * S // 2):(hf + 1) * (E * S // 2)]
                         ).then_inc(sSeT, 16)
        for m in range(MS):
            sy.wait_ge(sEo, 16 * m)
            if m >= 2:
                sy.wait_ge(sDeb, m - 1)
            sy.dma_start(out=eo_scr[m % 2][:], in_=eo_r[:, m, :]).then_inc(sEo, 16)
        sy.wait_ge(sOsn, 16 * 16)
        sy.wait_ge(sOsd, 16)
        for mc in range(8):
            sy.wait_ge(sSntc, 16 * mc)
            sy.wait_ge(sScd, 16 * mc)
            if mc >= 1:
                sy.wait_ge(sPeA, NTT * mc)
            sy.dma_start(out=sntc_scr[0][:], in_=snT_c[:, :, mc, :]).then_inc(sSntc, 16)
            sy.dma_start(out=schk[:],
                         in_=s_d.ap()[mc * 512:(mc + 1) * 512].partition_broadcast(128)
                         ).then_inc(sScd, 16)
        sy.wait_ge(sOA, 16)
        sy.dma_start(out=arep[:], in_=al_d.ap().partition_broadcast(128)).then_inc(sIn2, 16)
        for m in range(MS):
            sy.wait_ge(sSnt, 16 * m)
            if m >= 2:
                sy.wait_ge(sPeL, 32 * (m - 1))
            sy.dma_start(out=snt_scr[m % 2][:], in_=snT_b[:, :, m, :]).then_inc(sSnt, 16)
        sy.wait_ge(sODn, 16)
        sy.dma_start(out=denT[:], in_=den_d.ap().rearrange("(a p) -> p a", p=128),
                     allow_slow_non_contiguous=True).then_inc(sIn3, 16)
        for ts in range(4):
            for m in range(MS):
                i = ts * MS + m
                sy.wait_ge(sEdi, 16 * i)
                sy.wait_ge(sOE, 16 * (m + 1))
                if i >= 2:
                    sy.wait_ge(sPeC, i - 1)
                sy.dma_start(out=Ebl[i % 2][:], in_=Ed_r[:, m, ts * 512:(ts + 1) * 512]
                             ).then_inc(sEdi, 16)
                sy.wait_ge(sEoi, 16 * i)
                sy.wait_ge(sOB, 16 * (m + 1))
                sy.dma_start(out=eob2[i % 2][:], in_=Ebo_r[:, m, :]).then_inc(sEoi, 16)
        for tt in range(NTT):
            sy.wait_ge(sIn4, 16 * tt)
            if tt >= 2:
                sy.wait_ge(sVl, tt - 1)
            sy.dma_start(out=xres[tt % 2][:], in_=xm_r[:, tt, :]).then_inc(sIn4, 16)

    @blk.scalar
    def _(sc):
        sc.wait_ge(sIn, 96)
        for m in range(MS):
            sc.wait_ge(sSe, 16 * (m + 1))
            sc.activation(out=sq_scr[:], in_=se_scr[m % 2][:], func=AF.Square,
                          accum_out=ssq[:, m:m + 1]).then_inc(sAsq, 1)
        for tt in range(NTT):
            sc.activation(out=sq_scr[:], in_=xm[:, tt, :], func=AF.Square,
                          accum_out=sxq[:, tt:tt + 1]).then_inc(sAsq, 1)
        sc.wait_ge(sVc, 1)
        sc.activation(out=scr1[:], in_=ssq[:], func=AF.Ln).then_inc(sAc, 1)
        sc.wait_ge(sAc, 1)
        sc.activation(out=s_all[:], in_=scr1[:], func=AF.Exp, scale=-0.5).then_inc(sAc, 1)
        sc.wait_ge(sVc, 2)
        sc.activation(out=sxq[:], in_=sxq[:], func=AF.Ln).then_inc(sAc, 1)
        sc.wait_ge(sAc, 3)
        sc.activation(out=sx[:], in_=sxq[:], func=AF.Exp, scale=-0.5).then_inc(sAc, 1)
        for j in range(8):
            for tt in range(NTT):
                k = j * NTT + tt
                sc.wait_ge(sPeT, k + 1)
                sc.activation(out=xnT[:, j, tt * 128:(tt + 1) * 128], in_=p8[k % 2][:, 0:128],
                              func=AF.Identity, scale=ngc[:, j:j + 1]).then_inc(sAx, 1)
        for m in range(MS):
            for ts in range(4):
                g = 4 * m + ts
                if m >= 2:
                    sc.wait_ge(sDet, m - 1)
                sc.wait_ge(sPeL, 32 * m + 8 * (ts + 1))
                sc.activation(out=Lt[m % 2][:, ts * 512:(ts + 1) * 512],
                              in_=p8[g % 2][:], func=AF.Identity,
                              scale=s_all[:, m:m + 1]).then_inc(sAev, 1)
            sc.wait_ge(sDet, m + 1)
            if m >= 1:
                sc.wait_ge(sOE, 16 * m)
                sc.wait_ge(sPeDn, 4 * m)
            sc.activation(out=Etb[:], in_=Lt[m % 2][:], func=AF.Exp).then_inc(sAex, 1)
        for ts in range(4):
            for dj in range(8):
                sc.wait_ge(sPeC, MS * (ts + 1))
                sc.activation(out=xm[:].rearrange("p a d -> p (a d)")[:, dj * TLOC + ts * 512:dj * TLOC + (ts + 1) * 512],
                              in_=p8[dj][:], func=AF.Copy).then_inc(sAo, 1)
        for tt in range(NTT):
            sc.wait_ge(sVs, 2 + 2 * tt + 1)
            sc.activation(out=sq_scr[:], in_=xo[:], func=AF.Square,
                          accum_out=sumsq[:, tt:tt + 1]).then_inc(sAsq, 1)
            sc.wait_ge(sVs, 2 + 2 * tt + 2)
            sc.activation(out=lnv[:, tt:tt + 1], in_=var[:, tt:tt + 1], func=AF.Ln
                          ).then_inc(sAc, 1)
            sc.wait_ge(sAc, 4 + 2 * tt + 1)
            sc.activation(out=rstd[:, tt:tt + 1], in_=lnv[:, tt:tt + 1], func=AF.Exp,
                          scale=-0.5).then_inc(sAc, 1)

    @blk.vector
    def _(v):
        vn = [0]
        def ch(instr):
            instr.then_inc(sVc, 1)
            vn[0] += 1
            v.wait_ge(sVc, vn[0])
        v.wait_ge(sIn, 96)
        v.memset(eps_s[:], 1e-5)
        v.memset(den[:], 0.0)
        v.wait_ge(sAsq, MS)
        ch(v.tensor_scalar(out=ssq[:], in0=ssq[:], scalar1=1e-24, scalar2=None,
                           op0=Op.max, op1=Op.bypass))
        v.wait_ge(sAsq, MS + NTT)
        ch(v.tensor_scalar(out=sxq[:], in0=sxq[:], scalar1=1e-24, scalar2=None,
                           op0=Op.max, op1=Op.bypass))
        for jh in range(16):
            j = jh // 2
            v.wait_ge(sSeT, 16 * (jh + 1))
            ch(v.tensor_scalar(out=seT_scr[:], in0=seT_scr[:],
                               scalar1=sgc[:, j:j + 1], scalar2=None,
                               op0=Op.mult, op1=Op.bypass))
            if jh >= 1:
                v.wait_ge(sOsn, 16 * jh)
            ch(v.tensor_copy(sntb[:], seT_scr[:]))
        for m in range(MS):
            v.wait_ge(sEo, 16 * (m + 1))
            if m >= 1:
                v.wait_ge(sOB, 16 * m)
            v.tensor_copy(eob[:], eo_scr[m % 2][:]).then_inc(sDeb, 1)
        v.wait_ge(sAc, 4)
        for tt in range(NTT):
            v.tensor_scalar(out=xm[:, tt, :], in0=xm[:, tt, :],
                            scalar1=sx[:, tt:tt + 1], scalar2=None,
                            op0=Op.mult, op1=Op.bypass).then_inc(sDxn, 1)
        for mc in range(8):
            v.wait_ge(sSntc, 16 * (mc + 1))
            v.wait_ge(sScd, 16 * (mc + 1))
            for kd in range(8):
                ins = v.tensor_tensor(out=sntc_scr[0][:, kd, :], in0=sntc_scr[0][:, kd, :],
                                      in1=schk[:], op=Op.mult)
            ins.then_inc(sDsc, 1)
            for tt in range(NTT):
                k = mc * NTT + tt
                v.wait_ge(sPeA, k + 1)
                if mc == 0:
                    ch(v.tensor_scalar(out=asc[:], in0=p8[k % 2][:],
                                       scalar1=1.0, scalar2=None,
                                       op0=Op.mult, op1=Op.max,
                                       accum_out=alpha[:, tt:tt + 1]))
                else:
                    ch(v.tensor_scalar(out=asc[:], in0=p8[k % 2][:],
                                       scalar1=1.0, scalar2=alpha[:, tt:tt + 1],
                                       op0=Op.mult, op1=Op.max,
                                       accum_out=alpha[:, tt:tt + 1]))
                v.nop().then_inc(sDal, 1)
        ch(v.tensor_scalar(out=alpha[:], in0=alpha[:], scalar1=-1.0, scalar2=None,
                           op0=Op.mult, op1=Op.bypass))
        v.nop().then_inc(sVs, 1)
        v.wait_ge(sIn2, 16)
        for m in range(MS):
            v.wait_ge(sAev, 4 * (m + 1))
            v.tensor_tensor(out=Lt[m % 2][:], in0=Lt[m % 2][:], in1=arep[:],
                            op=Op.add).then_inc(sDet, 1)
            for ts in range(4):
                g = 4 * m + ts
                v.wait_ge(sPeDn, g + 1)
                v.scalar_tensor_tensor(out=den[0:1, ts * 512:(ts + 1) * 512],
                                       in0=p8[4 + (g % 2)][0:1, :], scalar=1.0,
                                       in1=den[0:1, ts * 512:(ts + 1) * 512],
                                       op0=Op.mult, op1=Op.add).then_inc(sDdn, 1)
        v.wait_ge(sIn3, 16)
        ch(v.reciprocal(rden[:], denT[:]))
        v.nop().then_inc(sVs, 1)
        for tt in range(NTT):
            if tt >= 1:
                v.wait_ge(sOX, 16 * tt)
            for dj in range(8):
                k = tt * 8 + dj
                v.wait_ge(sPeG, k + 1)
                v.tensor_scalar(out=xo[:, dj * 128:(dj + 1) * 128],
                                in0=p8[k % 2][:, 0:128], scalar1=rden[:, tt:tt + 1],
                                scalar2=None, op0=Op.mult, op1=Op.bypass).then_inc(sVg, 1)
            v.wait_ge(sIn4, 16 * (tt + 1))
            ch(v.tensor_tensor(out=xo[:], in0=xo[:], in1=xres[tt % 2][:], op=Op.add))
            v.nop().then_inc(sVs, 1)
            ch(v.reduce_sum(mu[:, tt:tt + 1], xo[:], axis=mybir.AxisListType.X))
            ch(v.tensor_scalar(out=mu[:, tt:tt + 1], in0=mu[:, tt:tt + 1],
                               scalar1=1.0 / D, scalar2=None, op0=Op.mult, op1=Op.bypass))
            ch(v.tensor_scalar(out=mu2[:, tt:tt + 1], in0=mu[:, tt:tt + 1],
                               scalar1=mu[:, tt:tt + 1], scalar2=None,
                               op0=Op.mult, op1=Op.bypass))
            v.wait_ge(sAsq, MS + NTT + tt + 1)
            ch(v.scalar_tensor_tensor(out=var[:, tt:tt + 1], in0=sumsq[:, tt:tt + 1],
                                      scalar=1.0 / D, in1=mu2[:, tt:tt + 1],
                                      op0=Op.mult, op1=Op.subtract))
            ch(v.tensor_scalar(out=var[:, tt:tt + 1], in0=var[:, tt:tt + 1],
                               scalar1=eps_s[:, 0:1], scalar2=None,
                               op0=Op.add, op1=Op.bypass))
            v.nop().then_inc(sVs, 1)
            ch(v.tensor_scalar(out=negmu[:, tt:tt + 1], in0=mu[:, tt:tt + 1],
                               scalar1=-1.0, scalar2=None, op0=Op.mult, op1=Op.bypass))
            v.wait_ge(sAc, 4 + 2 * tt + 2)
            ch(v.tensor_scalar(out=xo[:], in0=xo[:],
                               scalar1=negmu[:, tt:tt + 1], scalar2=rstd[:, tt:tt + 1],
                               op0=Op.add, op1=Op.mult))
            v.nop().then_inc(sVl, 1)

    @blk.tensor
    def _(tn):
        tn.wait_ge(sIn, 96)
        tn.wait_ge(sDxn, NTT)
        for j in range(8):
            for tt in range(NTT):
                k = j * NTT + tt
                if k >= 2:
                    tn.wait_ge(sAx, k - 1)
                tn.transpose(p8[k % 2][:, 0:128], xm[:, tt, j * 128:(j + 1) * 128],
                             eyef_s[:]).then_inc(sPeT, 1)
        tn.wait_ge(sAx, 128)
        for mc in range(8):
            tn.wait_ge(sDsc, mc + 1)
            for tt in range(NTT):
                k = mc * NTT + tt
                if k >= 2:
                    tn.wait_ge(sDal, k - 1)
                ins = None
                for kd in range(8):
                    ins = tn.matmul(p8[k % 2][:], xnT[:, kd, tt * 128:(tt + 1) * 128],
                                    sntc_scr[0][:, kd, :],
                                    start=(kd == 0), stop=(kd == 7))
                ins.then_inc(sPeA, 1)
        for m in range(MS):
            tn.wait_ge(sSnt, 16 * (m + 1))
            for ts in range(4):
                g = 4 * m + ts
                if g >= 2:
                    tn.wait_ge(sAev, g - 1)
                ins = None
                for kd in range(8):
                    ins = tn.matmul(p8[g % 2][:], snt_scr[m % 2][:, kd, :],
                                    xnT[:, kd, ts * 512:(ts + 1) * 512],
                                    start=(kd == 0), stop=(kd == 7))
                ins.then_inc(sPeL, 8)
            tn.wait_ge(sAex, m + 1)
            for ts in range(4):
                g = 4 * m + ts
                if g >= 2:
                    tn.wait_ge(sDdn, g - 1)
                tn.matmul(p8[4 + (g % 2)][0:1, :], onesc[:], Etb[:, ts * 512:(ts + 1) * 512],
                          start=True, stop=True).then_inc(sPeDn, 1)
        tn.wait_ge(sAev, 4 * MS)
        tn.wait_ge(sDdn, 4 * MS)
        for ts in range(4):
            for m in range(MS):
                i = ts * MS + m
                tn.wait_ge(sEdi, 16 * (i + 1))
                tn.wait_ge(sEoi, 16 * (i + 1))
                if m == 0 and ts >= 1:
                    tn.wait_ge(sAo, 8 * ts)
                ins = None
                for dj in range(8):
                    ins = tn.matmul(p8[dj][:], eob2[i % 2][:, dj * 128:(dj + 1) * 128],
                                    Ebl[i % 2][:], start=(m == 0), stop=(m == MS - 1))
                ins.then_inc(sPeC, 1)
        tn.wait_ge(sAo, 32)
        for tt in range(NTT):
            for dj in range(8):
                k = tt * 8 + dj
                if k >= 2:
                    tn.wait_ge(sVg, k - 1)
                tn.transpose(p8[k % 2][:, 0:128],
                             xm[:].rearrange("p a d -> p (a d)")[:, dj * TLOC + tt * 128:dj * TLOC + (tt + 1) * 128],
                             eyef_s[:]).then_inc(sPeG, 1)

    @blk.gpsimd
    def _(g):
        for jh in range(16):
            j, hf = jh // 2, jh % 2
            g.wait_ge(sOsn, 16 * jh)
            g.wait_ge(sVc, 2 + 2 * jh + 2)
            g.dma_start(out=snT_w[:, j, hf * (E * S // 2):(hf + 1) * (E * S // 2)],
                        in_=sntb[:]).then_inc(sOsn, 16)
        for m in range(MS):
            g.wait_ge(sOB, 16 * m)
            g.wait_ge(sDeb, m + 1)
            g.dma_start(out=Ebo_r[:, m, :], in_=eob[:]).then_inc(sOB, 16)
        g.wait_ge(sAc, 2)
        g.dma_start(out=s_d.ap().rearrange("(a p) -> p a", p=128), in_=s_all[:],
                    allow_slow_non_contiguous=True).then_inc(sOsd, 16)
        g.wait_ge(sVs, 1)
        g.dma_start(out=al_w, in_=alpha[:], allow_slow_non_contiguous=True).then_inc(sOA, 16)
        for m in range(MS):
            g.wait_ge(sOE, 16 * m)
            g.wait_ge(sAex, m + 1)
            g.dma_start(out=Ed_r[:, m, :], in_=Etb[:]).then_inc(sOE, 16)
        g.wait_ge(sDdn, 4 * MS)
        g.dma_start(out=den_d.ap(), in_=den[0:1, :]).then_inc(sODn, 16)
        for tt in range(NTT):
            g.wait_ge(sOX, 16 * tt)
            g.wait_ge(sVl, tt + 1)
            g.dma_start(out=o_r[:, tt, :], in_=xo[:]).then_inc(sOX, 16)
        g.wait_ge(sOX, 16 * NTT)

    es.close()
    return nc


def host_L6_maps(inp, r3, r5):
    ng = np.asarray(inp["L0_ffn_norm_gamma"], np.float32)
    sg = np.asarray(inp["L0_ffn_slot_norm_gamma"], np.float32)
    se = cc(np.asarray(inp["L0_ffn_slot_embeds"], np.float32).reshape(E * S, D))
    eo_all = cc(np.concatenate([np.asarray(r5.results[e]["eoT"]).T for e in range(NCORES)],
                               axis=0))
    ngs = ng * SCALE
    sgs = sg * SCALE
    onesc = np.ones((128, 1), dtype=BF)
    maps = []
    for j in range(NCORES):
        maps.append(dict(
            x_mid=np.asarray(r3.results[j]["x_mid"]),
            ng_cols=cc(ngs.reshape(8, 128).T),
            sg_cols=cc(sgs.reshape(8, 128).T),
            se=se, seT=cc(se.T), eo_all=eo_all,
            eyef=np.eye(128, dtype=np.float32), onesc=onesc,
        ))
    return maps



def kernel(**inp):
    cores = list(range(NCORES))
    r1 = run_bass_kernel_spmd(build_L1(), host_L1_maps(inp), core_ids=cores)
    r2 = run_bass_kernel_spmd(build_L2(), host_L2_maps(inp, r1), core_ids=cores)
    r3 = run_bass_kernel_spmd(build_L3(), host_L3_maps(inp, r1, r2), core_ids=cores)
    r4 = run_bass_kernel_spmd(build_L4(), host_L4_maps(inp, r3), core_ids=cores)
    r5 = run_bass_kernel_spmd(build_L5(), host_L5_maps(inp, r4), core_ids=cores)
    r6 = run_bass_kernel_spmd(build_L6(), host_L6_maps(inp, r3, r5), core_ids=cores)
    out = np.concatenate([np.asarray(r6.results[j]["out"], np.float32)
                          for j in range(NCORES)], axis=0)
    return out[None]


def _debug_L1(inp):
    return run_bass_kernel_spmd(build_L1(), host_L1_maps(inp),
                                core_ids=list(range(NCORES)))


def _debug_L2(inp, r1):
    return run_bass_kernel_spmd(build_L2(), host_L2_maps(inp, r1),
                                core_ids=list(range(NCORES)))


def _debug_L3(inp, r1, r2):
    return run_bass_kernel_spmd(build_L3(), host_L3_maps(inp, r1, r2),
                                core_ids=list(range(NCORES)))
